# version 02
# baseline (speedup 1.0000x reference)
import numpy as np
from contextlib import ExitStack

import concourse.bass as bass
import concourse.mybir as mybir
from concourse.bass_utils import run_bass_kernel_spmd

F32 = mybir.dt.float32
BF16 = mybir.dt.bfloat16
AF = mybir.ActivationFunctionType
ALU = mybir.AluOpType
AX = mybir.AxisListType

B, T, D = 4, 8192, 1024
NCORES = 8
NTOK = T // 2
TILE = 512
NTILE = NTOK // TILE
CH = 128
NCH = TILE // CH
DC = D // 128
M0 = 256
NEXP, FEXP = 16, 1024
SF = 4096
NH, DH = 16, 64
SCL0 = 1.0 / (16.0 * float(np.sqrt(T)))

ENGS = ("sync", "scalar", "vector", "tensor", "gpsimd")

CONST_VALS = [1e-8, 1e-5, -0.5, -1.0, 0.5, 2.0, 1.0 / D]


class Prog:

    def __init__(self, nc, stack):
        self.nc = nc
        self.stack = stack
        self.ops = {e: [] for e in ENGS}
        self.esem = {e: stack.enter_context(nc.semaphore(f"m_{e}")) for e in ENGS}
        self.cnt = {e: 0 for e in ENGS}
        self.dsem = {}
        self.dcnt = {}

    def _dma_sem(self, name):
        if name not in self.dsem:
            self.dsem[name] = self.stack.enter_context(self.nc.semaphore(f"d_{name}"))
            self.dcnt[name] = 0
        return self.dsem[name]

    def op(self, e, fn, mark=0):
        if mark:
            self.cnt[e] += mark
            sem = self.esem[e]

            def wrapped(eng, fn=fn, k=mark, sem=sem):
                inst = fn(eng)
                inst.then_inc(sem, k)

            self.ops[e].append(wrapped)
        else:
            self.ops[e].append(fn)
        return self.cnt[e]

    def mark(self, e):
        return self.cnt[e]

    def wait(self, e, dep_eng, val):
        if val <= 0:
            return
        sem = self.esem[dep_eng]
        self.ops[e].append(lambda eng, s=sem, v=val: eng.wait_ge(s, v))

    def dma(self, qe, out, in_, sem_name, **kw):
        sem = self._dma_sem(sem_name)
        self.dcnt[sem_name] += 16
        v = self.dcnt[sem_name]

        def f(eng, out=out, in_=in_, s=sem, kw=kw):
            eng.dma_start(out=out, in_=in_, **kw).then_inc(s, 16)

        self.ops[qe].append(f)
        return v

    def wait_dma(self, e, sem_name, val):
        if val <= 0:
            return
        sem = self._dma_sem(sem_name)
        self.ops[e].append(lambda eng, s=sem, v=val: eng.wait_ge(s, v))

    def emit(self, block):
        for e in ENGS:
            if not self.ops[e]:
                continue
            dec = getattr(block, e)

            def body(eng, e=e):
                for f in self.ops[e]:
                    f(eng)

            dec(body)


def fm(hbm_ap, t0=None, t1=None):
    r = hbm_ap.rearrange("(c p) t -> p c t", p=128)
    if t0 is None:
        return r
    return r[:, :, t0:t1]


def wlay(hbm_ap):
    return hbm_ap.rearrange("(c p) o -> p c o", p=128)


def col_layout(v):
    v = np.asarray(v)
    k = v.shape[0] // 128
    return np.ascontiguousarray(v.reshape(k, 128).T)


def host_arrays(inputs):
    g = {}
    g["wk_t"] = np.ascontiguousarray(inputs["L0_attn_Wk"].T)
    g["wv_t"] = np.ascontiguousarray(inputs["L0_attn_Wv"].T)
    g["wq_t"] = np.ascontiguousarray(inputs["L0_attn_Wq"].T)
    g["wg_col"] = col_layout(np.asarray(inputs["L0_attn_Wg"])[0])
    g["bg"] = np.ascontiguousarray(np.asarray(inputs["L0_attn_bg"]).reshape(1, 1))
    g["wo_t"] = np.ascontiguousarray(inputs["L0_attn_Wo"].T)
    g["router_t"] = np.ascontiguousarray(inputs["L0_ffn_router_w"].T)
    g["gup"] = np.ascontiguousarray(inputs["L0_ffn_gate_up_proj"])
    g["dwn"] = np.ascontiguousarray(inputs["L0_ffn_down_proj"])
    g["sgw_t"] = np.ascontiguousarray(inputs["L0_ffn_shared_gate_w"].T)
    g["suw_t"] = np.ascontiguousarray(inputs["L0_ffn_shared_up_w"].T)
    g["sdw_t"] = np.ascontiguousarray(inputs["L0_ffn_shared_down_w"].T)
    g["ln_g"] = col_layout(inputs["L1_attn_ln_g"])
    g["ln_b"] = col_layout(inputs["L1_attn_ln_b"])
    g["w_qkv"] = np.ascontiguousarray(inputs["L1_attn_w_qkv"])
    g["b_qkv_col"] = col_layout(inputs["L1_attn_b_qkv"])
    g["b_qkv_row"] = np.ascontiguousarray(np.asarray(inputs["L1_attn_b_qkv"]).reshape(1, 3 * 1024))
    g["w_gate"] = np.ascontiguousarray(inputs["L1_attn_w_gate"])
    g["b_gate"] = col_layout(inputs["L1_attn_b_gate"])
    g["w_proj"] = np.ascontiguousarray(inputs["L1_attn_w_proj"])
    g["b_proj"] = col_layout(inputs["L1_attn_b_proj"])
    g["wr_t"] = np.ascontiguousarray(inputs["L1_ffn_W_r"].T)
    g["wk1_t"] = np.ascontiguousarray(inputs["L1_ffn_W_k"].T)
    g["wv1_t"] = np.ascontiguousarray(inputs["L1_ffn_W_v"].T)
    g["mu_r"] = col_layout(inputs["L1_ffn_mu_r"])
    g["mu_k"] = col_layout(inputs["L1_ffn_mu_k"])
    g["sel"], g["selt"] = sel_arrays()
    g["u128"] = np.triu(np.ones((128, 128), np.float32))
    g["i128"] = np.eye(128, dtype=np.float32)
    g["i16"] = np.eye(16, dtype=np.float32)
    oh = np.zeros((16, 16 * 128), np.float32)
    for n in range(16):
        oh[n, n * 128:(n + 1) * 128] = 1.0
    g["oh16"] = oh
    g["consts"] = np.tile(np.asarray(CONST_VALS, np.float32)[None, :], (128, 1))
    x = np.asarray(inputs["L0_attn_x"])
    per = []
    for r in range(NCORES):
        b, h = r // 2, r % 2
        xs = np.ascontiguousarray(x[b, h * NTOK:(h + 1) * NTOK, :].T)
        mask = np.zeros((128, 8), np.float32)
        if h == 1:
            mask[:, r - 1] = 1.0
        per.append({"x_fm": xs, "agmask": mask})
    return g, per


def alloc_phaseA(nc, stack):
    e = stack.enter_context
    sb = {}
    sb["wk"] = e(nc.sbuf_tensor("A_wk", [128, DC, M0], BF16))
    sb["wv"] = e(nc.sbuf_tensor("A_wv", [128, DC, M0], BF16))
    sb["wq"] = e(nc.sbuf_tensor("A_wq", [128, DC, M0], BF16))
    sb["wg"] = e(nc.sbuf_tensor("A_wg", [128, DC], BF16))
    sb["u128"] = e(nc.sbuf_tensor("A_u128", [128, 128], F32))
    sb["i128"] = e(nc.sbuf_tensor("A_i128", [128, 128], F32))
    sb["consts"] = e(nc.sbuf_tensor("A_consts", [128, len(CONST_VALS)], F32))
    sb["bg"] = e(nc.sbuf_tensor("A_bg", [1, 1], F32))
    sb["negbg"] = e(nc.sbuf_tensor("A_negbg", [1, 1], F32))
    sb["ones_f32"] = e(nc.sbuf_tensor("A_ones_f32", [1, 128], F32))
    sb["ones_bf"] = e(nc.sbuf_tensor("A_ones_bf", [1, 128], BF16))
    sb["onescol_bf"] = e(nc.sbuf_tensor("A_onescol_bf", [128, 1], BF16))
    sb["sx"] = e(nc.sbuf_tensor("A_sx", [128, DC, TILE], F32))
    sb["sxbf"] = e(nc.sbuf_tensor("A_sxbf", [128, DC, TILE], BF16))
    sb["sxsq"] = e(nc.sbuf_tensor("A_sxsq", [128, DC, TILE], BF16))
    sb["sxnbf"] = e(nc.sbuf_tensor("A_sxnbf", [128, DC, TILE], BF16))
    sb["sxnf"] = e(nc.sbuf_tensor("A_sxnf", [128, DC, TILE], F32))
    sb["tmpc"] = e(nc.sbuf_tensor("A_tmpc", [128, TILE], F32))
    for n in ("mean", "var", "rstd", "nmr", "tmprow", "eg", "grow"):
        sb[n] = e(nc.sbuf_tensor(n, [1, TILE], F32))
    sb["lg"] = e(nc.sbuf_tensor("A_lg", [1, NTOK], F32))
    sb["sq"] = e(nc.sbuf_tensor("A_sq", [128, 2, TILE], BF16))
    sb["sv"] = e(nc.sbuf_tensor("A_sv", [128, 2, TILE], BF16))
    sb["skt"] = e(nc.sbuf_tensor("A_skt", [128, NCH, M0], BF16))
    sb["svt"] = e(nc.sbuf_tensor("A_svt", [128, NCH, M0], BF16))
    sb["qlc"] = e(nc.sbuf_tensor("A_qlc", [128, 2, TILE], BF16))
    sb["qlg"] = e(nc.sbuf_tensor("A_qlg", [128, 2, TILE], BF16))
    sb["srd"] = e(nc.sbuf_tensor("A_srd", [128, 2, TILE], BF16))
    sb["lgcol"] = e(nc.sbuf_tensor("A_lgcol", [128, NCH], F32))
    sb["cccol"] = e(nc.sbuf_tensor("A_cccol", [128, NCH], F32))
    sb["ccrow"] = e(nc.sbuf_tensor("A_ccrow", [1, TILE], F32))
    sb["ccgrow"] = e(nc.sbuf_tensor("A_ccgrow", [1, TILE], F32))
    sb["ccrow_cl"] = e(nc.sbuf_tensor("A_ccrow_cl", [1, TILE], F32))
    sb["ccgrow_cl"] = e(nc.sbuf_tensor("A_ccgrow_cl", [1, TILE], F32))
    sb["lamc"] = e(nc.sbuf_tensor("A_lamc", [1, TILE], F32))
    sb["lamg"] = e(nc.sbuf_tensor("A_lamg", [1, TILE], F32))
    sb["lamc_bf"] = e(nc.sbuf_tensor("A_lamc_bf", [1, TILE], BF16))
    sb["lamg_bf"] = e(nc.sbuf_tensor("A_lamg_bf", [1, TILE], BF16))
    sb["ccg"] = e(nc.sbuf_tensor("A_ccg", [1, 1], F32))
    sb["ecl"] = e(nc.sbuf_tensor("A_ecl", [128, NCH], F32))
    sb["ecol"] = e(nc.sbuf_tensor("A_ecol", [128, NCH], F32))
    sb["lamendl"] = e(nc.sbuf_tensor("A_lamendl", [128, NCH], F32))
    sb["lamend"] = e(nc.sbuf_tensor("A_lamend", [128, NCH], F32))
    sb["dl"] = e(nc.sbuf_tensor("A_dl", [128, 128], F32))
    sb["dexp"] = e(nc.sbuf_tensor("A_dexp", [128, 128], F32))
    sb["dmask"] = e(nc.sbuf_tensor("A_dmask", [128, 128], F32))
    sb["wbf"] = e(nc.sbuf_tensor("A_wbf", [128, 128], BF16))
    sb["vs"] = e(nc.sbuf_tensor("A_vs", [128, M0], BF16))
    sb["S_f32"] = e(nc.sbuf_tensor("A_S_f32", [128, 2, M0], F32))
    sb["S_bf"] = e(nc.sbuf_tensor("A_S_bf", [128, 2, M0], BF16))
    sb["s0out"] = e(nc.sbuf_tensor("A_s0out", [128, 2, M0], F32))
    sb["psRows"] = e(nc.psum_tensor("A_psRows", [128, TILE], F32))
    sb["psA"] = e(nc.psum_tensor("A_psA", [128, TILE], F32))
    sb["psB"] = e(nc.psum_tensor("A_psB", [128, TILE], F32))
    sb["psP1"] = e(nc.psum_tensor("A_psP1", [128, 2, TILE], F32))
    sb["psP2"] = e(nc.psum_tensor("A_psP2", [128, 2, TILE], F32))
    sb["psS1"] = e(nc.psum_tensor("A_psS1", [128, 12], F32))
    return sb


def emit_phaseA(p, io, sb):
    nc = p.nc
    psB1 = sb["psP1"][:, 0, 0:128]
    psSc = sb["psP1"][:, 0, 128:256]
    psRd = [sb["psP1"][:, 1, mk * 128:(mk + 1) * 128] for mk in range(2)]
    psDl = [sb["psP2"][:, mv, 0:M0] for mv in range(2)]
    psKt = sb["psP1"].rearrange("p c t -> p (c t)")
    psVt = sb["psP2"].rearrange("p c t -> p (c t)")

    lw = "lw"
    p.dma("gpsimd", sb["wk"][:], wlay(io["wk_t"]), lw)
    p.dma("gpsimd", sb["wv"][:], wlay(io["wv_t"]), lw)
    p.dma("gpsimd", sb["wq"][:], wlay(io["wq_t"]), lw)
    p.dma("gpsimd", sb["wg"][:], io["wg_col"], lw)
    p.dma("sync", sb["u128"][:], io["u128"], lw)
    p.dma("sync", sb["i128"][:], io["i128"], lw)
    p.dma("sync", sb["consts"][:], io["consts"], lw)
    p.dma("sync", sb["bg"][:], io["bg"], lw)
    lw_all = p.dcnt[lw]
    for e in ("tensor", "vector", "scalar"):
        p.wait_dma(e, lw, lw_all)

    def vpro(v):
        v.memset(sb["ones_f32"][:], 1.0)
        v.memset(sb["ones_bf"][:], 1.0)
        v.memset(sb["onescol_bf"][:], 1.0)
        v.memset(sb["S_f32"].rearrange("p c t -> p (c t)"), 0.0)
        v.memset(sb["S_bf"].rearrange("p c t -> p (c t)"), 0.0)
        v.memset(sb["ccg"][:], 0.0)
        return v.tensor_scalar_mul(sb["negbg"][:], sb["bg"][:], -1.0)

    vp = p.op("vector", vpro, mark=1)
    p.wait("tensor", "vector", vp)
    p.wait("scalar", "vector", vp)

    for t in range(NTILE):
        T0 = t * TILE
        if t > 0:
            p.wait_dma("vector", "sxn", p.dcnt.get("sxn", 0))
        vx = p.dma("sync", sb["sx"][:], fm(io["x_fm"], T0, T0 + TILE), "lx")
        p.wait_dma("vector", "lx", vx)

        v_cast = p.op("vector", lambda v: v.tensor_copy(
            sb["sxbf"].rearrange("p c t -> p (c t)"), sb["sx"].rearrange("p c t -> p (c t)")), mark=1)
        p.wait("scalar", "vector", v_cast)
        s_sq = p.op("scalar", lambda s: s.square(
            sb["sxsq"].rearrange("p c t -> p (c t)"), sb["sxbf"].rearrange("p c t -> p (c t)")), mark=1)

        p.wait("tensor", "vector", v_cast)
        p.wait("tensor", "scalar", s_sq)

        def stats(tn):
            for c in range(DC):
                tn.matmul(sb["psRows"][0:1, 0:TILE], sb["onescol_bf"][:], sb["sxbf"][:, c, :],
                          start=(c == 0), stop=(c == DC - 1))
            for c in range(DC):
                i = tn.matmul(sb["psRows"][32:33, 0:TILE], sb["onescol_bf"][:], sb["sxsq"][:, c, :],
                              start=(c == 0), stop=(c == DC - 1))
            return i

        t_stats = p.op("tensor", stats, mark=1)

        p.wait("vector", "tensor", t_stats)

        def rows1(v):
            v.tensor_scalar_mul(sb["mean"][:], sb["psRows"][0:1, 0:TILE], 1.0 / D)
            v.tensor_scalar_mul(sb["var"][:], sb["psRows"][32:33, 0:TILE], 1.0 / D)
            v.tensor_mul(sb["tmprow"][:], sb["mean"][:], sb["mean"][:])
            return v.tensor_sub(sb["var"][:], sb["var"][:], sb["tmprow"][:])

        v_rows1 = p.op("vector", rows1, mark=1)
        p.wait("scalar", "vector", v_rows1)

        def srows(s):
            s.activation(sb["tmprow"][:], sb["var"][:], AF.Ln, bias=1e-5)
            return s.activation(sb["rstd"][:], sb["tmprow"][:], AF.Exp, scale=-0.5)

        s_rows = p.op("scalar", srows, mark=1)
        p.wait("vector", "scalar", s_rows)

        def rows2(v):
            return v.scalar_tensor_tensor(sb["nmr"][:], sb["mean"][:], -1.0, sb["rstd"][:], ALU.mult, ALU.mult)

        v_rows2 = p.op("vector", rows2, mark=1)

        p.wait("tensor", "vector", v_rows2)

        def bc_ab(tn):
            tn.matmul(sb["psA"][:, 0:TILE], sb["ones_f32"][0:1, :], sb["rstd"][:], start=True, stop=True)
            return tn.matmul(sb["psB"][:, 0:TILE], sb["ones_f32"][0:1, :], sb["nmr"][:], start=True, stop=True)

        t_bc = p.op("tensor", bc_ab, mark=1)
        p.wait("vector", "tensor", t_bc)

        def ln_apply(v):
            for c in range(DC):
                v.tensor_tensor(sb["tmpc"][:, 0:TILE], sb["sx"][:, c, :], sb["psA"][:, 0:TILE], ALU.mult)
                v.tensor_tensor(sb["sxnf"][:, c, :], sb["tmpc"][:, 0:TILE], sb["psB"][:, 0:TILE], ALU.add)
            return v.tensor_copy(sb["sxnbf"].rearrange("p c t -> p (c t)"),
                                 sb["sxnf"].rearrange("p c t -> p (c t)"))

        v_ln = p.op("vector", ln_apply, mark=1)
        p.wait("sync", "vector", v_ln)
        p.dma("sync", fm(io["xn0s"], T0, T0 + TILE), sb["sxnf"][:], "sxn")

        p.wait("tensor", "vector", v_ln)

        def proj_qv(tn):
            for c in range(DC):
                tn.matmul(sb["psRows"][64:65, 0:TILE], sb["wg"][:, c:c + 1], sb["sxnbf"][:, c, :],
                          start=(c == 0), stop=(c == DC - 1))
            for mc in range(2):
                for c in range(DC):
                    tn.matmul(sb["psP1"][:, mc, :], sb["wq"][:, c, mc * 128:(mc + 1) * 128],
                              sb["sxnbf"][:, c, :], start=(c == 0), stop=(c == DC - 1))
            for mc in range(2):
                for c in range(DC):
                    i = tn.matmul(sb["psP2"][:, mc, :], sb["wv"][:, c, mc * 128:(mc + 1) * 128],
                                  sb["sxnbf"][:, c, :], start=(c == 0), stop=(c == DC - 1))
            return i

        t_qv = p.op("tensor", proj_qv, mark=1)
        p.wait("scalar", "tensor", t_qv)
        s_eg = p.op("scalar", lambda s: s.activation(
            sb["eg"][:], sb["psRows"][64:65, 0:TILE], AF.Exp,
            scale=-1.0, bias=sb["negbg"][0:1, 0:1]), mark=1)
        p.wait("vector", "scalar", s_eg)

        def v_egrow(v):
            v.tensor_scalar_add(sb["eg"][:], sb["eg"][:], 1.0)
            return v.reciprocal(sb["grow"][:], sb["eg"][:])

        v_eg = p.op("vector", v_egrow, mark=1)
        p.wait("scalar", "vector", v_eg)
        s_lg = p.op("scalar", lambda s, T0=T0: s.activation(
            sb["lg"][0:1, T0:T0 + TILE], sb["grow"][:], AF.Ln, bias=1e-8), mark=1)
        p.wait("vector", "tensor", t_qv)

        def evac_qv(v):
            v.tensor_copy(sb["sq"].rearrange("p c t -> p (c t)"), sb["psP1"].rearrange("p c t -> p (c t)"))
            return v.tensor_copy(sb["sv"].rearrange("p c t -> p (c t)"), sb["psP2"].rearrange("p c t -> p (c t)"))

        v_qv = p.op("vector", evac_qv, mark=1)

        p.wait("tensor", "vector", v_qv)

        def proj_kv(tn):
            for tc in range(NCH):
                for c in range(DC):
                    tn.matmul(psKt[:, tc * M0:(tc + 1) * M0],
                              sb["sxnbf"][:, c, tc * 128:(tc + 1) * 128], sb["wk"][:, c, :],
                              start=(c == 0), stop=(c == DC - 1))
            for tc in range(NCH):
                for c in range(DC):
                    i = tn.matmul(psVt[:, tc * M0:(tc + 1) * M0],
                                  sb["sxnbf"][:, c, tc * 128:(tc + 1) * 128], sb["wv"][:, c, :],
                                  start=(c == 0), stop=(c == DC - 1))
            return i

        t_kv = p.op("tensor", proj_kv, mark=1)
        p.wait("vector", "tensor", t_kv)

        def evac_kv(v):
            v.tensor_copy(sb["skt"].rearrange("p c t -> p (c t)"), psKt)
            return v.tensor_copy(sb["svt"].rearrange("p c t -> p (c t)"), psVt)

        v_kv = p.op("vector", evac_kv, mark=1)

        p.wait("tensor", "scalar", s_lg)
        p.wait("tensor", "vector", v_kv)

        t_p1 = p.op("tensor", lambda tn, T0=T0: [
            tn.matmul(sb["psS1"][:, cc:cc + 1], sb["lg"][0:1, T0 + cc * CH:T0 + (cc + 1) * CH],
                      sb["ones_f32"][0:1, 0:1], start=True, stop=True) for cc in range(NCH)][-1], mark=1)
        p.wait("vector", "tensor", t_p1)
        v_p1 = p.op("vector", lambda v: v.tensor_copy(sb["lgcol"][:], sb["psS1"][:, 0:NCH]), mark=1)
        p.wait("tensor", "vector", v_p1)

        t_p2 = p.op("tensor", lambda tn: [
            tn.matmul(sb["psS1"][:, 4 + cc:5 + cc], sb["u128"][:], sb["lgcol"][:, cc:cc + 1],
                      start=True, stop=True) for cc in range(NCH)][-1], mark=1)
        p.wait("vector", "tensor", t_p2)
        v_p2 = p.op("vector", lambda v: v.tensor_copy(sb["cccol"][:], sb["psS1"][:, 4:4 + NCH]), mark=1)
        p.wait("tensor", "vector", v_p2)

        t_p3 = p.op("tensor", lambda tn: [
            tn.matmul(sb["psRows"][0:1, cc * CH:(cc + 1) * CH], sb["cccol"][:, cc:cc + 1], sb["i128"][:],
                      start=True, stop=True) for cc in range(NCH)][-1], mark=1)
        p.wait("vector", "tensor", t_p3)

        p.op("vector", lambda v: v.tensor_copy(sb["ccrow"][:], sb["psRows"][0:1, 0:TILE]))
        for cc in range(NCH):
            vg = p.op("vector", lambda v, cc=cc: v.tensor_scalar(
                sb["ccgrow"][0:1, cc * CH:(cc + 1) * CH],
                sb["ccrow"][0:1, cc * CH:(cc + 1) * CH],
                sb["ccg"][0:1, 0:1], None, ALU.add), mark=1)
            p.wait("scalar", "vector", vg)
            sg = p.op("scalar", lambda s, cc=cc: s.copy(
                sb["ccg"][:], sb["ccgrow"][0:1, cc * CH + CH - 1:cc * CH + CH]), mark=1)
            p.wait("vector", "scalar", sg)

        def vrows(v):
            v.tensor_scalar(sb["ccrow_cl"][:], sb["ccrow"][:], 0.0, None, ALU.min)
            return v.tensor_scalar(sb["ccgrow_cl"][:], sb["ccgrow"][:], 0.0, None, ALU.min)

        v_rows3 = p.op("vector", vrows, mark=1)
        p.wait("scalar", "vector", v_rows3)

        def slam(s):
            s.activation(sb["lamc"][:], sb["ccrow_cl"][:], AF.Exp)
            return s.activation(sb["lamg"][:], sb["ccgrow_cl"][:], AF.Exp)

        s_lam = p.op("scalar", slam, mark=1)
        p.wait("vector", "scalar", s_lam)
        v_lambf = p.op("vector", lambda v: (
            v.tensor_copy(sb["lamc_bf"][:], sb["lamc"][:]),
            v.tensor_copy(sb["lamg_bf"][:], sb["lamg"][:]))[-1], mark=1)
        p.wait("tensor", "vector", v_lambf)

        def bc_lam(tn):
            tn.matmul(sb["psA"][:, 0:TILE], sb["ones_bf"][0:1, :], sb["lamc_bf"][:], start=True, stop=True)
            return tn.matmul(sb["psB"][:, 0:TILE], sb["ones_bf"][0:1, :], sb["lamg_bf"][:], start=True, stop=True)

        t_lam = p.op("tensor", bc_lam, mark=1)
        p.wait("vector", "tensor", t_lam)

        def qlam_mul(v):
            for mc in range(2):
                v.tensor_tensor(sb["qlc"][:, mc, :], sb["sq"][:, mc, :], sb["psA"][:, 0:TILE], ALU.mult)
            for mc in range(2):
                i = v.tensor_tensor(sb["qlg"][:, mc, :], sb["sq"][:, mc, :], sb["psB"][:, 0:TILE], ALU.mult)
            return i

        v_qlam = p.op("vector", qlam_mul, mark=1)
        p.wait("sync", "vector", v_qlam)
        p.dma("sync", io["qlam_s"].rearrange("(c p) t -> p c t", p=128)[:, :, T0:T0 + TILE],
              sb["qlg"][:], "sxn")

        t_end = p.op("tensor", lambda tn: [
            tn.matmul(sb["psS1"][:, 8 + cc:9 + cc], sb["ones_f32"][0:1, :],
                      sb["ccrow"][0:1, cc * CH + CH - 1:cc * CH + CH], start=True, stop=True)
            for cc in range(NCH)][-1], mark=1)
        p.wait("vector", "tensor", t_end)

        def v_ecl(v):
            for cc in range(NCH):
                v.tensor_scalar(sb["ecl"][:, cc:cc + 1], sb["psS1"][:, 8 + cc:9 + cc],
                                sb["cccol"][:, cc:cc + 1], 0.0, ALU.subtract, ALU.min)
            return v.tensor_scalar(sb["lamendl"][:], sb["psS1"][:, 8:8 + NCH], 0.0, None, ALU.min)

        v_ecl_m = p.op("vector", v_ecl, mark=1)
        p.wait("scalar", "vector", v_ecl_m)

        def s_ecol(s):
            s.activation(sb["ecol"][:], sb["ecl"][:], AF.Exp)
            return s.activation(sb["lamend"][:], sb["lamendl"][:], AF.Exp)

        s_ec = p.op("scalar", s_ecol, mark=1)
        p.wait("vector", "scalar", s_ec)
        p.wait("tensor", "vector", v_qlam)

        for cc in range(NCH):
            j0 = cc * CH

            def t_step1(tn, cc=cc, j0=j0):
                tn.matmul(psB1, sb["ones_f32"][0:1, :], sb["ccrow"][0:1, j0:j0 + CH],
                          start=True, stop=True)
                for mc in range(2):
                    i = tn.matmul(psSc, sb["sv"][:, mc, j0:j0 + CH],
                                  sb["sq"][:, mc, j0:j0 + CH], start=(mc == 0), stop=(mc == 1))
                return i

            t1 = p.op("tensor", t_step1, mark=1)
            p.wait("vector", "tensor", t1)
            v1 = p.op("vector", lambda v, cc=cc: v.tensor_scalar(
                sb["dl"][:], psB1, sb["cccol"][:, cc:cc + 1], 0.0, ALU.subtract, ALU.min), mark=1)
            p.wait("scalar", "vector", v1)
            s1 = p.op("scalar", lambda s: s.activation(sb["dexp"][:], sb["dl"][:], AF.Exp), mark=1)
            p.wait("vector", "scalar", s1)

            def v_step2(v):
                v.tensor_tensor(sb["dmask"][:], sb["dexp"][:], sb["u128"][:], ALU.mult)
                return v.tensor_tensor(sb["wbf"][:], psSc, sb["dmask"][:], ALU.mult)

            v2 = p.op("vector", v_step2, mark=1)
            p.wait("tensor", "vector", v2)

            def t_step2(tn, cc=cc, j0=j0):
                for mk in range(2):
                    tn.matmul(psRd[mk], sb["skt"][:, cc, mk * 128:(mk + 1) * 128],
                              sb["wbf"][:], start=True, stop=False)
                    for mv in range(2):
                        i = tn.matmul(psRd[mk], sb["S_bf"][:, mv, mk * 128:(mk + 1) * 128],
                                      sb["qlc"][:, mv, j0:j0 + CH], start=False, stop=(mv == 1))
                return i

            t2 = p.op("tensor", t_step2, mark=1)
            p.wait("vector", "tensor", t2)

            def v_step3(v, cc=cc, j0=j0):
                for mk in range(2):
                    v.tensor_scalar_mul(sb["srd"][:, mk, j0:j0 + CH], psRd[mk], SCL0)
                return v.tensor_scalar_mul(sb["vs"][:], sb["svt"][:, cc, :], sb["ecol"][:, cc:cc + 1])

            v3 = p.op("vector", v_step3, mark=1)
            p.wait("tensor", "vector", v3)

            def t_step3(tn, cc=cc):
                for mv in range(2):
                    i = tn.matmul(psDl[mv], sb["vs"][:, mv * 128:(mv + 1) * 128],
                                  sb["skt"][:, cc, :], start=True, stop=True)
                return i

            t3 = p.op("tensor", t_step3, mark=1)
            p.wait("vector", "tensor", t3)

            def v_step4(v, cc=cc):
                for mv in range(2):
                    v.scalar_tensor_tensor(sb["S_f32"][:, mv, :], sb["S_f32"][:, mv, :],
                                           sb["lamend"][:, cc:cc + 1], psDl[mv],
                                           ALU.mult, ALU.add)
                return v.tensor_copy(sb["S_bf"].rearrange("p c t -> p (c t)"),
                                     sb["S_f32"].rearrange("p c t -> p (c t)"))

            v4 = p.op("vector", v_step4, mark=1)
            p.wait("tensor", "vector", v4)

        vr = p.mark("vector")
        p.wait("sync", "vector", vr)
        p.dma("sync", io["reads_s"].rearrange("(c p) t -> p c t", p=128)[:, :, T0:T0 + TILE],
              sb["srd"][:], "sxn")
        p.wait("sync", "vector", v_ln)

    vS = p.op("vector", lambda v: v.tensor_copy(sb["s0out"].rearrange("p c t -> p (c t)"),
                                                sb["S_f32"].rearrange("p c t -> p (c t)")), mark=1)
    p.wait("sync", "vector", vS)
    p.dma("sync", io["s0"].rearrange("(c p) t -> p c t", p=128), sb["s0out"][:], "sxn")
    if "lg_dbg" in io:
        p.dma("sync", io["lg_dbg"], sb["lg"][:], "sxn")
    p.wait_dma("sync", "sxn", p.dcnt["sxn"])


A_INPUT_SHAPES = [("x_fm", [D, NTOK]), ("wk_t", [D, M0]), ("wv_t", [D, M0]), ("wq_t", [D, M0]),
                  ("wg_col", [128, DC]), ("bg", [1, 1]), ("u128", [128, 128]), ("i128", [128, 128]),
                  ("consts", [128, len(CONST_VALS)])]


def build_A(dbg=False):
    nc = bass.Bass("TRN2", target_bir_lowering=False)
    io = {}
    for n, shp in A_INPUT_SHAPES:
        io[n] = nc.dram_tensor(n, shp, F32, kind="ExternalInput").ap()
    io["xn0s"] = nc.dram_tensor("xn0s", [D, NTOK], F32, kind="ExternalOutput").ap()
    io["reads_s"] = nc.dram_tensor("reads_s", [M0, NTOK], BF16, kind="ExternalOutput").ap()
    io["qlam_s"] = nc.dram_tensor("qlam_s", [M0, NTOK], BF16, kind="ExternalOutput").ap()
    io["s0"] = nc.dram_tensor("s0", [M0, M0], F32, kind="ExternalOutput").ap()
    if dbg:
        io["lg_dbg"] = nc.dram_tensor("lg_dbg", [1, NTOK], F32, kind="ExternalOutput").ap()
    with ExitStack() as stack:
        sb = alloc_phaseA(nc, stack)
        for i, v in enumerate(CONST_VALS):
            nc.const_aps.register(F32, v, sb["consts"][:, i:i + 1])
        p = Prog(nc, stack)
        emit_phaseA(p, io, sb)
        block = stack.enter_context(nc.Block())
        p.emit(block)
    return nc


TB = 1024
NBLK = NTOK // TB
NST = TB // TILE


def alloc_phaseB(nc, stack):
    e = stack.enter_context
    sb = {}
    sb["consts"] = e(nc.sbuf_tensor("B_consts", [128, len(CONST_VALS)], F32))
    sb["i16"] = e(nc.sbuf_tensor("B_i16", [16, 16], F32))
    sb["srow"] = e(nc.sbuf_tensor("B_srow", [1, TILE], F32))
    sb["i128"] = e(nc.sbuf_tensor("B_i128", [128, 128], F32))
    sb["ones_f32"] = e(nc.sbuf_tensor("B_ones_f32", [1, 128], F32))
    sb["router"] = e(nc.sbuf_tensor("B_router", [128, DC, NEXP], F32))
    sb["wo"] = e(nc.sbuf_tensor("B_wo", [128, 2, D], BF16))
    sb["s0f"] = e(nc.sbuf_tensor("B_s0f", [128, 2, M0], F32))
    sb["s0b"] = e(nc.sbuf_tensor("B_s0b", [128, 2, M0], BF16))
    sb["xnf"] = e(nc.sbuf_tensor("B_xnf", [128, DC, TILE], F32))
    sb["xn"] = e(nc.sbuf_tensor("B_xn", [128, DC, TB], BF16))
    sb["routed"] = e(nc.sbuf_tensor("B_routed", [128, DC, TB], F32))
    sb["act"] = e(nc.sbuf_tensor("B_act", [128, DC, TB], BF16))
    sb["gup2"] = [e(nc.sbuf_tensor(f"B_gup{i}", [128, DC, 2 * FEXP], BF16)) for i in range(2)]
    sb["dwn2"] = [e(nc.sbuf_tensor(f"B_dwn{i}", [128, DC, FEXP], BF16)) for i in range(2)]
    sb["un1"] = e(nc.sbuf_tensor("B_un1", [128, TB], F32))
    sb["ltok"] = e(nc.sbuf_tensor("B_ltok", [128, DC, 16], F32))
    sb["mx"] = e(nc.sbuf_tensor("B_mx", [128, DC], F32))
    sb["msk"] = e(nc.sbuf_tensor("B_msk", [128, DC, 16], F32))
    sb["th"] = e(nc.sbuf_tensor("B_th", [128, DC, 16], F32))
    sb["sig"] = e(nc.sbuf_tensor("B_sig", [128, DC, 16], F32))
    sb["sctok"] = e(nc.sbuf_tensor("B_sctok", [128, DC, 16], F32))
    sb["scfm"] = e(nc.sbuf_tensor("B_scfm", [16, TB], F32))
    sb["scbc"] = e(nc.sbuf_tensor("B_scbc", [128, TILE], F32))
    sb["sil"] = e(nc.sbuf_tensor("B_sil", [128, TILE], F32))
    sb["rds"] = e(nc.sbuf_tensor("B_rds", [128, 2, TILE], BF16))
    sb["qlm"] = e(nc.sbuf_tensor("B_qlm", [128, 2, TILE], BF16))
    sb["psR1"] = e(nc.psum_tensor("B_psR1", [16, TB], F32))
    sb["psR2"] = e(nc.psum_tensor("B_psR2", [128, 128], F32))
    sb["psSb"] = e(nc.psum_tensor("B_psSb", [128, TILE], F32))
    sb["psGU"] = e(nc.psum_tensor("B_psGU", [128, 2, TILE], F32))
    sb["psD"] = e(nc.psum_tensor("B_psD", [128, TILE], F32))
    return sb


def emit_phaseB(p, io, sb):
    sb["sx"] = sb["xnf"]
    sb["lfm"] = sb["un1"][0:16, :]
    sb["t1"] = sb["un1"][:, 0:TILE]
    sb["t3"] = sb["un1"][:, TILE:2 * TILE]
    sb["rt"] = sb["rds"]
    sb["tres"] = sb["t1"]
    psG = sb["psGU"][:, 0, :]
    psU = sb["psGU"][:, 1, :]
    psC = [sb["psGU"][:, mk, :] for mk in range(2)]

    lw = "lwB"
    p.dma("sync", sb["consts"][:], io["consts"], lw)
    p.dma("sync", sb["i16"][:], io["i16"], lw)
    p.dma("sync", sb["i128"][:], io["i128"], lw)
    p.dma("sync", sb["router"][:], wlay(io["router_t"]), lw)
    p.dma("sync", sb["s0f"][:], io["s0prev"].rearrange("(c p) t -> p c t", p=128), lw)
    p.dma("gpsimd", sb["wo"][:], wlay(io["wo_t"]), lw)
    lw_all = p.dcnt[lw]
    for e in ("tensor", "vector", "scalar"):
        p.wait_dma(e, lw, lw_all)

    def vpro(v):
        v.memset(sb["ones_f32"][:], 1.0)
        return v.tensor_copy(sb["s0b"].rearrange("p c t -> p (c t)"),
                             sb["s0f"].rearrange("p c t -> p (c t)"))

    vp = p.op("vector", vpro, mark=1)
    p.wait("tensor", "vector", vp)

    for kb in range(NBLK):
        KB0 = kb * TB
        p.wait_dma("vector", "sx1", p.dcnt.get("sx1", 0))
        for st in range(NST):
            T0 = KB0 + st * TILE
            if st > 0:
                p.wait("sync", "tensor", p.mark("tensor"))
                p.wait("sync", "vector", p.mark("vector"))
            p.wait_dma("sync", "sx1", p.dcnt.get("sx1", 0))
            vx = p.dma("sync", sb["xnf"][:], fm(io["xn0s"], T0, T0 + TILE), "lxB")
            p.wait_dma("vector", "lxB", vx)
            v_cast = p.op("vector", lambda v, st=st: v.tensor_copy(
                sb["xn"][:, :, st * TILE:(st + 1) * TILE].rearrange("p c t -> p c t"),
                sb["xnf"].rearrange("p c t -> p c t")), mark=1)
            p.wait("tensor", "vector", v_cast)
            p.wait_dma("tensor", "lxB", vx)

            def r_logits(tn, st=st):
                for c in range(DC):
                    i = tn.matmul(sb["psR1"][0:16, st * TILE:(st + 1) * TILE], sb["router"][:, c, :],
                                  sb["xnf"][:, c, :], start=(c == 0), stop=(c == DC - 1))
                return i

            t_r1 = p.op("tensor", r_logits, mark=1)
        p.wait("vector", "tensor", t_r1)
        v_r1 = p.op("vector", lambda v: v.tensor_copy(sb["lfm"], sb["psR1"][0:16, :]), mark=1)
        p.wait("tensor", "vector", v_r1)

        def r_tok(tn):
            for tc in range(DC):
                i = tn.matmul(sb["psR2"][:, tc * 16:(tc + 1) * 16],
                              sb["lfm"][:, tc * 128:(tc + 1) * 128], sb["i16"][:],
                              start=True, stop=True)
            return i

        t_r2 = p.op("tensor", r_tok, mark=1)
        p.wait("vector", "tensor", t_r2)

        def v_route(v):
            v.tensor_copy(sb["ltok"].rearrange("p c t -> p (c t)"), sb["psR2"][:])
            for tc in range(DC):
                v.reduce_max(sb["mx"][:, tc:tc + 1], sb["ltok"][:, tc, :], axis=AX.X)
            for tc in range(DC):
                i = v.tensor_scalar(sb["msk"][:, tc, :], sb["ltok"][:, tc, :],
                                    sb["mx"][:, tc:tc + 1], None, ALU.is_ge)
            return i

        v_r2 = p.op("vector", v_route, mark=1)
        p.wait("scalar", "vector", v_r2)
        s_r = p.op("scalar", lambda s: s.activation(
            sb["th"].rearrange("p c t -> p (c t)"), sb["ltok"].rearrange("p c t -> p (c t)"),
            AF.Tanh, scale=0.5), mark=1)
        p.wait("vector", "scalar", s_r)

        def v_score(v):
            v.tensor_scalar(sb["sig"].rearrange("p c t -> p (c t)"),
                            sb["th"].rearrange("p c t -> p (c t)"), 0.5, 0.5, ALU.mult, ALU.add)
            return v.tensor_tensor(sb["sctok"].rearrange("p c t -> p (c t)"),
                                   sb["sig"].rearrange("p c t -> p (c t)"),
                                   sb["msk"].rearrange("p c t -> p (c t)"), ALU.mult)

        v_r3 = p.op("vector", v_score, mark=1)
        p.wait("tensor", "vector", v_r3)

        def r_back(tn):
            for tc in range(DC):
                i = tn.matmul(sb["psR1"][0:16, tc * 128:(tc + 1) * 128],
                              sb["sctok"][:, tc, :], sb["i128"][:], start=True, stop=True)
            return i

        t_r3 = p.op("tensor", r_back, mark=1)
        p.wait("vector", "tensor", t_r3)
        v_r4 = p.op("vector", lambda v: v.tensor_copy(sb["scfm"][:], sb["psR1"][0:16, :]), mark=1)

        p.wait("gpsimd", "tensor", p.mark("tensor"))
        lg0 = p.dma("gpsimd", sb["gup2"][0][:], wlay(io["gup"][0]), "lgup")
        ld0 = p.dma("gpsimd", sb["dwn2"][0][:], wlay(io["dwn"][0]), "ldwn")
        p.wait("tensor", "vector", v_r4)
        for n in range(NEXP):
            buf = n % 2
            p.wait_dma("tensor", "lgup", 16 * (kb * NEXP + n + 1))
            for st in range(NST):
                S0 = st * TILE

                def t_row(tn, n=n, S0=S0):
                    return tn.matmul(sb["psSb"][0:1, :], sb["i16"][:, n:n + 1],
                                     sb["scfm"][0:16, S0:S0 + TILE], start=True, stop=True)

                t_rw = p.op("tensor", t_row, mark=1)
                p.wait("vector", "tensor", t_rw)
                v_rw = p.op("vector", lambda v: v.tensor_copy(sb["srow"][:], sb["psSb"][0:1, :]), mark=1)
                p.wait("tensor", "vector", v_rw)
                t_sb = p.op("tensor", lambda tn: tn.matmul(
                    sb["psSb"][:], sb["ones_f32"][0:1, :], sb["srow"][:], start=True, stop=True), mark=1)
                p.wait("vector", "tensor", t_sb)
                v_sb = p.op("vector", lambda v: v.tensor_copy(sb["scbc"][:], sb["psSb"][:]), mark=1)
                for f in range(DC):

                    def t_gu(tn, n=n, buf=buf, f=f, S0=S0):
                        for c in range(DC):
                            tn.matmul(psG, sb["gup2"][buf][:, c, f * 128:(f + 1) * 128],
                                      sb["xn"][:, c, S0:S0 + TILE], start=(c == 0), stop=(c == DC - 1))
                        for c in range(DC):
                            i = tn.matmul(psU, sb["gup2"][buf][:, c, FEXP + f * 128:FEXP + (f + 1) * 128],
                                          sb["xn"][:, c, S0:S0 + TILE], start=(c == 0), stop=(c == DC - 1))
                        return i

                    t_g = p.op("tensor", t_gu, mark=1)
                    p.wait("vector", "tensor", t_g)

                    def v_e1(v):
                        v.tensor_tensor(sb["t1"], psG, sb["scbc"][:], ALU.mult)
                        return v.tensor_tensor(sb["t3"], psU, sb["scbc"][:], ALU.mult)

                    v1 = p.op("vector", v_e1, mark=1)
                    p.wait("scalar", "vector", v1)
                    s1 = p.op("scalar", lambda s: s.activation(sb["sil"][:], sb["t1"], AF.Silu), mark=1)
                    p.wait("vector", "scalar", s1)
                    v2 = p.op("vector", lambda v, f=f, S0=S0: v.tensor_tensor(
                        sb["act"][:, f, S0:S0 + TILE], sb["sil"][:], sb["t3"], ALU.mult), mark=1)
                    p.wait("tensor", "vector", v2)
            if n + 1 < NEXP:
                p.wait("gpsimd", "tensor", p.mark("tensor"))
                p.dma("gpsimd", sb["gup2"][(n + 1) % 2][:], wlay(io["gup"][n + 1]), "lgup")
            p.wait_dma("tensor", "ldwn", 16 * (kb * NEXP + n + 1))
            for st in range(NST):
                S0 = st * TILE
                for h in range(DC):
                    ps = psG if h % 2 == 0 else psU

                    def t_d(tn, n=n, buf=buf, h=h, S0=S0, ps=ps):
                        for c in range(DC):
                            i = tn.matmul(ps, sb["dwn2"][buf][:, c, h * 128:(h + 1) * 128],
                                          sb["act"][:, c, S0:S0 + TILE], start=(c == 0), stop=(c == DC - 1))
                        return i

                    t_dm = p.op("tensor", t_d, mark=1)
                    p.wait("vector", "tensor", t_dm)
                    if n == 0:
                        v_d = p.op("vector", lambda v, h=h, S0=S0, ps=ps: v.tensor_copy(
                            sb["routed"][:, h, S0:S0 + TILE], ps), mark=1)
                    else:
                        v_d = p.op("vector", lambda v, h=h, S0=S0, ps=ps: v.tensor_tensor(
                            sb["routed"][:, h, S0:S0 + TILE], sb["routed"][:, h, S0:S0 + TILE],
                            ps, ALU.add), mark=1)
                    p.wait("tensor", "vector", v_d)
            if n + 1 < NEXP:
                p.wait("gpsimd", "tensor", p.mark("tensor"))
                p.dma("gpsimd", sb["dwn2"][(n + 1) % 2][:], wlay(io["dwn"][n + 1]), "ldwn")

        for fh in range(2):
            F0 = fh * 2048
            p.wait("gpsimd", "tensor", p.mark("tensor"))
            lg = p.dma("gpsimd", sb["gup2"][0][:], wlay(io["sgw_t"][:, F0:F0 + 2048]), "lsh")
            lg = p.dma("gpsimd", sb["gup2"][1][:], wlay(io["suw_t"][:, F0:F0 + 2048]), "lsh")
            p.wait_dma("tensor", "lsh", lg)
            for fo in range(16):
                for st in range(NST):
                    S0 = st * TILE

                    def t_sgu(tn, fo=fo, S0=S0):
                        for c in range(DC):
                            tn.matmul(psG, sb["gup2"][0][:, c, fo * 128:(fo + 1) * 128],
                                      sb["xn"][:, c, S0:S0 + TILE], start=(c == 0), stop=(c == DC - 1))
                        for c in range(DC):
                            i = tn.matmul(psU, sb["gup2"][1][:, c, fo * 128:(fo + 1) * 128],
                                          sb["xn"][:, c, S0:S0 + TILE], start=(c == 0), stop=(c == DC - 1))
                        return i

                    t_s1 = p.op("tensor", t_sgu, mark=1)
                    p.wait("scalar", "tensor", t_s1)
                    s_s1 = p.op("scalar", lambda s: s.activation(sb["sil"][:], psG, AF.Silu), mark=1)
                    p.wait("vector", "scalar", s_s1)
                    tdst = (sb["act"][:, fo, S0:S0 + TILE] if fo < DC
                            else sb["dwn2"][1][:, fo - DC, S0:S0 + TILE])
                    v_s1 = p.op("vector", lambda v, tdst=tdst: v.tensor_tensor(
                        tdst, sb["sil"][:], psU, ALU.mult), mark=1)
                    p.wait("tensor", "vector", v_s1)
            for sh in range(2):
                R0 = F0 + sh * 1024
                p.wait("gpsimd", "tensor", p.mark("tensor"))
                ls = p.dma("gpsimd", sb["dwn2"][0][:], wlay(io["sdw_t"][R0:R0 + 1024, :]), "lsd")
                p.wait_dma("tensor", "lsd", ls)
                for st in range(NST):
                    S0 = st * TILE
                    for h in range(DC):
                        ps = psG if h % 2 == 0 else psU

                        def t_sd(tn, sh=sh, h=h, S0=S0, ps=ps):
                            for c in range(DC):
                                fk = sh * DC + c
                                src = (sb["act"][:, fk, S0:S0 + TILE] if fk < DC
                                       else sb["dwn2"][1][:, fk - DC, S0:S0 + TILE])
                                i = tn.matmul(ps, sb["dwn2"][0][:, c, h * 128:(h + 1) * 128],
                                              src, start=(c == 0), stop=(c == DC - 1))
                            return i

                        t_dm = p.op("tensor", t_sd, mark=1)
                        p.wait("vector", "tensor", t_dm)
                        v_d = p.op("vector", lambda v, h=h, S0=S0, ps=ps: v.tensor_tensor(
                            sb["routed"][:, h, S0:S0 + TILE], sb["routed"][:, h, S0:S0 + TILE],
                            ps, ALU.add), mark=1)
                        p.wait("tensor", "vector", v_d)

        if "dbg_scfm" in io:
            vsc = p.mark("vector")
            p.wait("sync", "vector", vsc)
            p.dma("sync", io["dbg_scfm"][:, KB0:KB0 + TB], sb["scfm"][0:16, :], "sx1")
            p.dma("sync", io["dbg_routed"].rearrange("(c p) t -> p c t", p=128)[:, :, KB0:KB0 + TB],
                  sb["routed"][:], "sx1")
        for st in range(NST):
            T0 = KB0 + st * TILE
            p.wait("sync", "tensor", p.mark("tensor"))
            p.wait("sync", "vector", p.mark("vector"))
            vx2 = p.dma("sync", sb["sx"][:], fm(io["x_fm"], T0, T0 + TILE), "lxB")
            vr2 = p.dma("sync", sb["rds"][:], io["reads_s"].rearrange("(c p) t -> p c t", p=128)[:, :, T0:T0 + TILE], "lxB")
            vq2 = p.dma("sync", sb["qlm"][:], io["qlam_s"].rearrange("(c p) t -> p c t", p=128)[:, :, T0:T0 + TILE], "lxB")
            p.wait_dma("tensor", "lxB", vq2)

            def t_corr(tn):
                for mk in range(2):
                    for mv in range(2):
                        i = tn.matmul(psC[mk], sb["s0b"][:, mv, mk * 128:(mk + 1) * 128],
                                      sb["qlm"][:, mv, :], start=(mv == 0), stop=(mv == 1))
                return i

            t_c = p.op("tensor", t_corr, mark=1)
            p.wait("vector", "tensor", t_c)
            p.wait_dma("vector", "lxB", vq2)

            def v_rt(v):
                for mk in range(2):
                    i = v.scalar_tensor_tensor(sb["rt"][:, mk, :], psC[mk], SCL0,
                                               sb["rds"][:, mk, :], ALU.mult, ALU.add)
                return i

            v_c = p.op("vector", v_rt, mark=1)
            p.wait("tensor", "vector", v_c)

            for dc in range(DC):
                ps = psG if dc % 2 == 0 else psU

                def t_o(tn, dc=dc, ps=ps):
                    for mkc in range(2):
                        i = tn.matmul(ps, sb["wo"][:, mkc, dc * 128:(dc + 1) * 128],
                                      sb["rt"][:, mkc, :], start=(mkc == 0), stop=(mkc == 1))
                    return i

                t_om = p.op("tensor", t_o, mark=1)
                p.wait("vector", "tensor", t_om)

                def v_x1(v, dc=dc, ps=ps, st=st):
                    v.tensor_tensor(sb["tres"], sb["sx"][:, dc, :], ps, ALU.add)
                    return v.tensor_tensor(sb["sx"][:, dc, :], sb["tres"],
                                           sb["routed"][:, dc, st * TILE:(st + 1) * TILE], ALU.add)

                v_o = p.op("vector", v_x1, mark=1)
                p.wait("tensor", "vector", v_o)
            vf = p.mark("vector")
            p.wait("sync", "vector", vf)
            p.dma("sync", fm(io["x1s"], T0, T0 + TILE), sb["sx"][:], "sx1")

    p.wait_dma("sync", "sx1", p.dcnt["sx1"])


B_INPUT_SHAPES = [("x_fm", [D, NTOK], F32), ("xn0s", [D, NTOK], F32),
                  ("reads_s", [M0, NTOK], BF16), ("qlam_s", [M0, NTOK], BF16),
                  ("s0prev", [M0, M0], F32), ("wo_t", [M0, D], F32),
                  ("router_t", [D, NEXP], F32), ("gup", [NEXP, D, 2 * FEXP], F32),
                  ("dwn", [NEXP, FEXP, D], F32), ("sgw_t", [D, SF], F32),
                  ("suw_t", [D, SF], F32), ("sdw_t", [SF, D], F32),
                  ("i16", [16, 16], F32), ("i128", [128, 128], F32),
                  ("consts", [128, len(CONST_VALS)], F32)]


def build_B(dbg=False):
    nc = bass.Bass("TRN2", target_bir_lowering=False)
    io = {}
    for n, shp, dt in B_INPUT_SHAPES:
        io[n] = nc.dram_tensor(n, shp, dt, kind="ExternalInput").ap()
    io["x1s"] = nc.dram_tensor("x1s", [D, NTOK], F32, kind="ExternalOutput").ap()
    if dbg:
        io["dbg_scfm"] = nc.dram_tensor("dbg_scfm", [16, NTOK], F32, kind="ExternalOutput").ap()
        io["dbg_routed"] = nc.dram_tensor("dbg_routed", [D, NTOK], F32, kind="ExternalOutput").ap()
    with ExitStack() as stack:
        sb = alloc_phaseB(nc, stack)
        for i, v in enumerate(CONST_VALS):
            nc.const_aps.register(F32, v, sb["consts"][:, i:i + 1])
        p = Prog(nc, stack)
        emit_phaseB(p, io, sb)
        block = stack.enter_context(nc.Block())
        p.emit(block)
    return nc


STW = 1024 + 8 + 8


def alloc_phaseL1(nc, stack, full):
    e = stack.enter_context
    P = "C_" if full else "P_"
    sb = {}
    sb["consts"] = e(nc.sbuf_tensor(P + "consts", [128, len(CONST_VALS)], F32))
    sb["i128b"] = e(nc.sbuf_tensor(P + "i128b", [128, 128], BF16))
    sb["u128"] = e(nc.sbuf_tensor(P + "u128", [128, 128], F32))
    sb["u128b"] = e(nc.sbuf_tensor(P + "u128b", [128, 128], BF16))
    sb["ones_f32"] = e(nc.sbuf_tensor(P + "ones_f32", [1, 128], F32))
    sb["onescol_bf"] = e(nc.sbuf_tensor(P + "onescol_bf", [128, 1], BF16))
    sb["wqkvK"] = e(nc.sbuf_tensor(P + "wqkvK", [128, DC, D], BF16))
    sb["wqkvV"] = e(nc.sbuf_tensor(P + "wqkvV", [128, DC, D], BF16))
    if full:
        sb["wstr"] = [e(nc.sbuf_tensor(P + f"wstr{i}", [128, DC, D], BF16)) for i in range(2)]
        sb["wgate"] = sb["wstr"][0]
        sb["wqkvQ"] = sb["wstr"][1]
        sb["wproj"] = sb["wstr"][0]
        sb["wr"] = sb["wstr"][1]
        sb["wk1"] = sb["wstr"][0]
        sb["wv1"] = sb["wstr"][1]
        sb["sel"] = e(nc.sbuf_tensor(P + "sel", [128, DC * 16], F32))
        sb["selt"] = e(nc.sbuf_tensor(P + "selt", [16, DC * 128], F32))
        sb["mu_r"] = e(nc.sbuf_tensor(P + "mu_r", [128, DC], F32))
        sb["mu_k"] = e(nc.sbuf_tensor(P + "mu_k", [128, DC], F32))
        sb["bproj"] = e(nc.sbuf_tensor(P + "bproj", [128, DC], F32))
        sb["bq"] = e(nc.sbuf_tensor(P + "bq", [128, DC], F32))
    else:
        sb["wgate"] = e(nc.sbuf_tensor(P + "wgate", [128, DC, D], BF16))
    sb["lng"] = e(nc.sbuf_tensor(P + "lng", [128, DC], F32))
    sb["lnb"] = e(nc.sbuf_tensor(P + "lnb", [128, DC], F32))
    sb["bk"] = e(nc.sbuf_tensor(P + "bk", [128, DC], F32))
    sb["bgate"] = e(nc.sbuf_tensor(P + "bgate", [128, DC], F32))
    sb["nbgate"] = e(nc.sbuf_tensor(P + "nbgate", [128, DC], F32))
    sb["bqkv_row"] = e(nc.sbuf_tensor(P + "bqkv_row", [1, 3 * D], F32))
    sb["bvbc"] = e(nc.sbuf_tensor(P + "bvbc", [128, D], F32))
    sb["x1f"] = e(nc.sbuf_tensor(P + "x1f", [128, DC, TILE], F32))
    sb["xn1f"] = e(nc.sbuf_tensor(P + "xn1f", [128, DC, TILE], F32))
    sb["xn1b"] = e(nc.sbuf_tensor(P + "xn1b", [128, DC, TILE], BF16))
    sb["xln"] = e(nc.sbuf_tensor(P + "xln", [128, DC, TILE], BF16))
    sb["gsig"] = e(nc.sbuf_tensor(P + "gsig", [128, DC, TILE], F32))
    if full:
        sb["gn"] = sb["xn1f"]
    else:
        sb["gn"] = sb["gsig"]
    sb["kf"] = e(nc.sbuf_tensor(P + "kf", [128, DC, TILE], BF16))
    if full:
        sb["x1b"] = sb["kf"]
    else:
        sb["x1b"] = e(nc.sbuf_tensor(P + "x1b", [128, DC, TILE], BF16))
    sb["kt"] = e(nc.sbuf_tensor(P + "kt", [128, NCH, D], BF16))
    sb["vt"] = e(nc.sbuf_tensor(P + "vt", [128, NCH, D], BF16))
    sb["tc1"] = e(nc.sbuf_tensor(P + "tc1", [128, TILE], F32))
    sb["tc2"] = e(nc.sbuf_tensor(P + "tc2", [128, TILE], F32))
    for n in ("mean", "var", "rstd", "tmprow"):
        sb[n] = e(nc.sbuf_tensor(P + n, [1, TILE], F32))
    sb["grow"] = sb["tmprow"]
    sb["nmr"] = sb["var"]
    if not full:
        sb["x1sq"] = e(nc.sbuf_tensor(P + "x1sq", [128, DC, TILE], BF16))
    sb["stf"] = e(nc.sbuf_tensor(P + "stf", [128, 1024], F32))
    sb["ksum"] = e(nc.sbuf_tensor(P + "ksum", [128, DC], F32))
    sb["kst"] = e(nc.sbuf_tensor(P + "kst", [128, DC], F32))
    sb["last"] = e(nc.sbuf_tensor(P + "last", [128, DC], F32))
    if full:
        sb["qf"] = e(nc.sbuf_tensor(P + "qf", [128, DC, TILE], BF16))
        sb["ao"] = e(nc.sbuf_tensor(P + "ao", [128, DC, TILE], BF16))
        sb["x1sq"] = sb["ao"]
        sb["stb"] = sb["xln"][:, 1:3, :].rearrange("p a b -> p (a b)")
        sb["w4"] = sb["xln"]
        sb["dinv"] = sb["tc1"][0:16, 0:128]
        sb["dh"] = sb["tc2"][:, 0:128]
        sb["carry"] = e(nc.sbuf_tensor(P + "carry", [128, DC], BF16))
        sb["xrb"] = sb["qf"]
        sb["xkb"] = sb["ao"]
        sb["kab"] = sb["kf"]
    sb["ps1"] = e(nc.psum_tensor(P + "ps1", [128, TILE], F32))
    sb["ps2"] = e(nc.psum_tensor(P + "ps2", [128, TILE], F32))
    sb["ps3"] = e(nc.psum_tensor(P + "ps3", [128, 2, TILE], F32))
    if full:
        sb["ps4"] = e(nc.psum_tensor(P + "ps4", [128, 2, TILE], F32))
    sb["ps5"] = e(nc.psum_tensor(P + "ps5", [128, TILE], F32))
    if full:
        sb["ps6"] = sb["ps1"]
    else:
        sb["ps6"] = e(nc.psum_tensor(P + "ps6", [128, TILE], F32))
    sb["psT"] = e(nc.psum_tensor(P + "psT", [128, DC, 128], BF16))
    return sb


def emit_l1_prologue(p, io, sb, full):
    lw = "lwL"
    p.dma("sync", sb["consts"][:], io["consts"], lw)
    p.dma("gpsimd", sb["i128b"][:], io["i128"], lw)
    p.dma("sync", sb["u128"][:], io["u128"], lw)
    p.dma("sync", sb["lng"][:], io["ln_g"], lw)
    p.dma("sync", sb["lnb"][:], io["ln_b"], lw)
    p.dma("sync", sb["bk"][:], io["b_qkv_col"][:, DC:2 * DC], lw)
    p.dma("sync", sb["bgate"][:], io["b_gate"], lw)
    p.dma("sync", sb["bqkv_row"][:], io["b_qkv_row"], lw)
    p.dma("gpsimd", sb["wqkvK"][:], wlay(io["w_qkv"][:, D:2 * D]), lw)
    p.dma("gpsimd", sb["wqkvV"][:], wlay(io["w_qkv"][:, 2 * D:3 * D]), lw)
    if not full:
        p.dma("gpsimd", sb["wgate"][:], wlay(io["w_gate"]), lw)
    if full:
        p.dma("sync", sb["sel"][:], io["sel"], lw)
        p.dma("sync", sb["selt"][:], io["selt"], lw)
        p.dma("sync", sb["mu_r"][:], io["mu_r"], lw)
        p.dma("sync", sb["mu_k"][:], io["mu_k"], lw)
        p.dma("sync", sb["bproj"][:], io["b_proj"], lw)
        p.dma("sync", sb["bq"][:], io["b_qkv_col"][:, 0:DC], lw)
        p.dma("sync", sb["stf"][:], io["stprev"][:, 0:1024], lw)
        p.dma("sync", sb["ksum"][:], io["stprev"][:, 1024:1032], lw)
        p.dma("gpsimd", sb["carry"][:], io["stprev"][:, 1032:1040], lw)
    lw_all = p.dcnt[lw]
    for e in ("tensor", "vector", "scalar"):
        p.wait_dma(e, lw, lw_all)

    def vpro(v):
        v.memset(sb["ones_f32"][:], 1.0)
        v.memset(sb["onescol_bf"][:], 1.0)
        v.tensor_copy(sb["u128b"][:], sb["u128"][:])
        v.tensor_scalar_mul(sb["nbgate"][:], sb["bgate"][:], -1.0)
        if not full:
            v.memset(sb["stf"][:], 0.0)
            v.memset(sb["ksum"][:], 0.0)
        if full:
            v.tensor_copy(sb["stb"][:], sb["stf"][:])
        return v.memset(sb["tc1"][:], 0.0)

    vp = p.op("vector", vpro, mark=1)
    p.wait("tensor", "vector", vp)
    p.wait("scalar", "vector", vp)

    def t_bv(tn):
        for h in range(2):
            i = tn.matmul(sb["ps1"][:, 0:TILE], sb["ones_f32"][0:1, :],
                          sb["bqkv_row"][0:1, 2 * D + h * TILE:2 * D + (h + 1) * TILE],
                          start=True, stop=True)
            p_dummy = i
        return i

    for h in range(2):
        t_b = p.op("tensor", lambda tn, h=h: tn.matmul(
            sb["ps1"][:, 0:TILE], sb["ones_f32"][0:1, :],
            sb["bqkv_row"][0:1, 2 * D + h * TILE:2 * D + (h + 1) * TILE],
            start=True, stop=True), mark=1)
        p.wait("vector", "tensor", t_b)
        v_b = p.op("vector", lambda v, h=h: v.tensor_copy(
            sb["bvbc"][:, h * TILE:(h + 1) * TILE], sb["ps1"][:, 0:TILE]), mark=1)
        p.wait("tensor", "vector", v_b)


def emit_l1_pre(p, io, sb, t, full):
    T0 = t * TILE
    pp = [sb["ps1"][:, 0:TILE], sb["ps2"][:, 0:TILE]]

    p.wait_dma("sync", "sout", p.dcnt.get("sout", 0))
    p.wait("sync", "vector", p.mark("vector"))
    p.wait("sync", "tensor", p.mark("tensor"))
    if full:
        p.wait("gpsimd", "tensor", p.mark("tensor"))
        lw_g = p.dma("gpsimd", sb["wgate"][:], wlay(io["w_gate"]), "lws")
        lw_q = p.dma("gpsimd", sb["wqkvQ"][:], wlay(io["w_qkv"][:, 0:D]), "lws")
        sb["_lw_g"], sb["_lw_q"] = lw_g, lw_q
    vx = p.dma("sync", sb["x1f"][:], fm(io["x1s"], T0, T0 + TILE), "lx1")
    p.wait_dma("vector", "lx1", vx)
    v_cast = p.op("vector", lambda v: v.tensor_copy(
        sb["x1b"].rearrange("p c t -> p (c t)"), sb["x1f"].rearrange("p c t -> p (c t)")), mark=1)
    p.wait("scalar", "vector", v_cast)
    s_sq = p.op("scalar", lambda s: s.square(
        sb["x1sq"].rearrange("p c t -> p (c t)"), sb["x1b"].rearrange("p c t -> p (c t)")), mark=1)
    p.wait("tensor", "vector", v_cast)
    p.wait("tensor", "scalar", s_sq)

    def ln_stats(src_b, src_sq):
        def stats(tn):
            for c in range(DC):
                tn.matmul(sb["ps1"][0:1, :], sb["onescol_bf"][:], src_b[:, c, :],
                          start=(c == 0), stop=(c == DC - 1))
            for c in range(DC):
                i = tn.matmul(sb["ps1"][32:33, :], sb["onescol_bf"][:], src_sq[:, c, :],
                              start=(c == 0), stop=(c == DC - 1))
            return i
        return stats

    t_st = p.op("tensor", ln_stats(sb["x1b"], sb["x1sq"]), mark=1)
    p.wait("vector", "tensor", t_st)

    def rows1(v):
        v.tensor_scalar_mul(sb["mean"][:], sb["ps1"][0:1, :], 1.0 / D)
        v.tensor_scalar_mul(sb["var"][:], sb["ps1"][32:33, :], 1.0 / D)
        v.tensor_mul(sb["tmprow"][:], sb["mean"][:], sb["mean"][:])
        return v.tensor_sub(sb["var"][:], sb["var"][:], sb["tmprow"][:])

    v_r1 = p.op("vector", rows1, mark=1)
    p.wait("scalar", "vector", v_r1)

    def srows(s):
        s.activation(sb["tmprow"][:], sb["var"][:], AF.Ln, bias=1e-5)
        return s.activation(sb["rstd"][:], sb["tmprow"][:], AF.Exp, scale=-0.5)

    s_r = p.op("scalar", srows, mark=1)
    p.wait("vector", "scalar", s_r)
    v_r2 = p.op("vector", lambda v: v.scalar_tensor_tensor(
        sb["nmr"][:], sb["mean"][:], -1.0, sb["rstd"][:], ALU.mult, ALU.mult), mark=1)
    p.wait("tensor", "vector", v_r2)

    def bc_ab(tn):
        tn.matmul(sb["ps2"][:], sb["ones_f32"][0:1, :], sb["rstd"][:], start=True, stop=True)
        return tn.matmul(sb["ps5"][:], sb["ones_f32"][0:1, :], sb["nmr"][:], start=True, stop=True)

    t_bc = p.op("tensor", bc_ab, mark=1)
    p.wait("vector", "tensor", t_bc)

    def ln1_apply(v):
        for c in range(DC):
            v.tensor_tensor(sb["tc1"][:], sb["x1f"][:, c, :], sb["ps2"][:], ALU.mult)
            v.tensor_tensor(sb["xn1f"][:, c, :], sb["tc1"][:], sb["ps5"][:], ALU.add)
        return v.tensor_copy(sb["xn1b"].rearrange("p c t -> p (c t)"),
                             sb["xn1f"].rearrange("p c t -> p (c t)"))

    v_ln1 = p.op("vector", ln1_apply, mark=1)
    p.wait("scalar", "vector", v_ln1)
    s_sq2 = p.op("scalar", lambda s: s.square(
        sb["x1sq"].rearrange("p c t -> p (c t)"), sb["xn1b"].rearrange("p c t -> p (c t)")), mark=1)
    p.wait("tensor", "vector", v_ln1)
    p.wait("tensor", "scalar", s_sq2)
    t_st2 = p.op("tensor", ln_stats(sb["xn1b"], sb["x1sq"]), mark=1)
    p.wait("vector", "tensor", t_st2)
    v_r1b = p.op("vector", rows1, mark=1)
    p.wait("scalar", "vector", v_r1b)
    s_r2 = p.op("scalar", srows, mark=1)
    p.wait("vector", "scalar", s_r2)
    v_r2b = p.op("vector", lambda v: v.scalar_tensor_tensor(
        sb["nmr"][:], sb["mean"][:], -1.0, sb["rstd"][:], ALU.mult, ALU.mult), mark=1)
    p.wait("tensor", "vector", v_r2b)
    t_bc2 = p.op("tensor", bc_ab, mark=1)
    p.wait("vector", "tensor", t_bc2)

    def ln2_apply(v):
        for c in range(DC):
            v.tensor_tensor(sb["tc1"][:], sb["xn1f"][:, c, :], sb["ps2"][:], ALU.mult)
            v.tensor_tensor(sb["tc2"][:], sb["tc1"][:], sb["ps5"][:], ALU.add)
            i = v.tensor_scalar(sb["xln"][:, c, :], sb["tc2"][:],
                                sb["lng"][:, c:c + 1], sb["lnb"][:, c:c + 1], ALU.mult, ALU.add)
        return i

    v_ln2 = p.op("vector", ln2_apply, mark=1)
    p.wait("tensor", "vector", v_ln2)

    if full:
        p.wait_dma("tensor", "lws", sb["_lw_g"])
    for co in range(DC):
        ps = pp[co % 2]

        def t_g(tn, co=co, ps=ps):
            for c in range(DC):
                i = tn.matmul(ps, sb["wgate"][:, c, co * 128:(co + 1) * 128],
                              sb["xn1b"][:, c, :], start=(c == 0), stop=(c == DC - 1))
            return i

        t_gm = p.op("tensor", t_g, mark=1)
        p.wait("scalar", "tensor", t_gm)
        s_g = p.op("scalar", lambda s, co=co, ps=ps: s.activation(
            sb["tc1"][:], ps, AF.Exp, scale=-1.0, bias=sb["nbgate"][:, co:co + 1]), mark=1)
        p.wait("vector", "scalar", s_g)

        def v_g(v, co=co):
            v.tensor_scalar_add(sb["tc1"][:], sb["tc1"][:], 1.0)
            return v.reciprocal(sb["gsig"][:, co, :], sb["tc1"][:])

        v_gm = p.op("vector", v_g, mark=1)
        p.wait("tensor", "vector", v_gm)
    v_gsb = p.op("vector", lambda v: v.tensor_copy(
        sb["x1sq"].rearrange("p c t -> p (c t)"), sb["gsig"].rearrange("p c t -> p (c t)")), mark=1)
    p.wait("tensor", "vector", v_gsb)
    t_gs = p.op("tensor", lambda tn: [
        tn.matmul(sb["ps1"][0:1, :], sb["onescol_bf"][:], sb["x1sq"][:, c, :],
                  start=(c == 0), stop=(c == DC - 1)) for c in range(DC)][-1], mark=1)
    p.wait("vector", "tensor", t_gs)

    def v_gd(v):
        v.tensor_scalar(sb["mean"][:], sb["ps1"][0:1, :], 1.0 / D, 1e-5, ALU.mult, ALU.add)
        return v.reciprocal(sb["grow"][:], sb["mean"][:])

    v_gdm = p.op("vector", v_gd, mark=1)
    p.wait("tensor", "vector", v_gdm)
    t_gb = p.op("tensor", lambda tn: tn.matmul(
        sb["ps2"][:], sb["ones_f32"][0:1, :], sb["grow"][:], start=True, stop=True), mark=1)
    p.wait("vector", "tensor", t_gb)
    v_gn = p.op("vector", lambda v: [
        v.tensor_tensor(sb["gn"][:, c, :], sb["gsig"][:, c, :], sb["ps2"][:], ALU.mult)
        for c in range(DC)][-1], mark=1)
    p.wait("tensor", "vector", v_gn)

    def proj_gated(wname, bcol, dst):
        for co in range(DC):
            ps = pp[co % 2]

            def t_k(tn, co=co, ps=ps, wname=wname):
                for c in range(DC):
                    i = tn.matmul(ps, sb[wname][:, c, co * 128:(co + 1) * 128],
                                  sb["xln"][:, c, :], start=(c == 0), stop=(c == DC - 1))
                return i

            t_km = p.op("tensor", t_k, mark=1)
            p.wait("vector", "tensor", t_km)

            def v_k1(v, co=co, ps=ps, bcol=bcol):
                v.tensor_scalar(sb["tc1"][:], ps, sb[bcol][:, co:co + 1], None, ALU.add)
                v.tensor_tensor(sb["tc2"][:], sb["tc1"][:], sb["gn"][:, co, :], ALU.mult)
                return v.tensor_scalar(sb["tc1"][:], sb["tc2"][:], 0.0, None, ALU.min)

            v_k1m = p.op("vector", v_k1, mark=1)
            p.wait("scalar", "vector", v_k1m)
            s_k = p.op("scalar", lambda s: s.activation(sb["tc1"][:], sb["tc1"][:], AF.Exp), mark=1)
            p.wait("vector", "scalar", s_k)

            def v_k2(v, co=co, dst=dst):
                v.tensor_scalar(sb["tc2"][:], sb["tc2"][:], 0.0, None, ALU.max)
                return v.tensor_tensor(sb[dst][:, co, :], sb["tc1"][:], sb["tc2"][:], ALU.add)

            v_k2m = p.op("vector", v_k2, mark=1)
            p.wait("tensor", "vector", v_k2m)

    proj_gated("wqkvK", "bk", "kf")
    if full:
        p.wait_dma("tensor", "lws", sb["_lw_q"])
        proj_gated("wqkvQ", "bq", "qf")

    for tc in range(NCH):
        def t_tr(tn, tc=tc):
            for c in range(DC):
                i = tn.transpose(sb["psT"][:, c, :], sb["kf"][:, c, tc * 128:(tc + 1) * 128],
                                 sb["i128b"][:])
            return i

        t_t = p.op("tensor", t_tr, mark=1)
        p.wait("vector", "tensor", t_t)
        v_t = p.op("vector", lambda v, tc=tc: v.tensor_copy(
            sb["kt"][:, tc, :], sb["psT"].rearrange("p c t -> p (c t)")), mark=1)
        p.wait("tensor", "vector", v_t)

    for tc in range(NCH):
        def t_v(tn, tc=tc):
            for h in range(2):
                for c in range(DC):
                    i = tn.matmul(sb["ps3"][:, h, :], sb["xln"][:, c, tc * 128:(tc + 1) * 128],
                                  sb["wqkvV"][:, c, h * TILE:(h + 1) * TILE],
                                  start=(c == 0), stop=(c == DC - 1))
            return i

        t_vm = p.op("tensor", t_v, mark=1)
        p.wait("vector", "tensor", t_vm)

        def v_v(v, tc=tc):
            for h in range(2):
                i = v.tensor_tensor(sb["vt"][:, tc, h * TILE:(h + 1) * TILE], sb["ps3"][:, h, :],
                                    sb["bvbc"][:, h * TILE:(h + 1) * TILE], ALU.add)
            return i

        v_vm = p.op("vector", v_v, mark=1)
        p.wait("tensor", "vector", v_vm)


def emit_phaseBP(p, io, sb):
    emit_l1_prologue(p, io, sb, full=False)
    for t in range(NTILE):
        emit_l1_pre(p, io, sb, t, full=False)
        for tc in range(NCH):
            g_ch = t * NCH + tc

            def t_stacc(tn, tc=tc, g_ch=g_ch):
                for h in range(NH):
                    r0 = 64 * (h % 2)
                    c0 = 64 * (h // 2)
                    i = tn.matmul(sb["ps6"][r0:r0 + 64, c0:c0 + 64],
                                  sb["kt"][:, tc, h * 64:(h + 1) * 64],
                                  sb["vt"][:, tc, h * 64:(h + 1) * 64],
                                  start=(g_ch == 0), stop=(g_ch == NTILE * NCH - 1))
                return i

            p.op("tensor", t_stacc, mark=1)
        def v_ks(v, t=t):
            for c in range(DC):
                v.reduce_sum(sb["kst"][:, c:c + 1], sb["kf"][:, c, :], axis=AX.X)
            return v.tensor_tensor(sb["ksum"][:], sb["ksum"][:], sb["kst"][:], ALU.add)

        p.op("vector", v_ks, mark=1)
        if t == NTILE - 1:
            v_lt = p.op("vector", lambda v: v.tensor_copy(
                sb["last"][:], sb["xn1f"][:, :, TILE - 1:TILE].rearrange("p c o -> p (c o)")), mark=1)
    tm = p.mark("tensor")
    p.wait("vector", "tensor", tm)

    def v_st_f(v):
        v.memset(sb["stf"][:], 0.0)
        for h in range(NH):
            r0 = 64 * (h % 2)
            c0 = 64 * (h // 2)
            cb = h // 2
            i = v.tensor_copy(sb["stf"][r0:r0 + 64, cb * 128 + r0:cb * 128 + r0 + 64],
                              sb["ps6"][r0:r0 + 64, c0:c0 + 64])
        return i

    v_st = p.op("vector", v_st_f, mark=1)
    p.wait("sync", "vector", v_st)
    p.dma("sync", io["st_out"][:, 0:1024], sb["stf"][:], "sst")
    p.dma("sync", io["st_out"][:, 1024:1032], sb["ksum"][:], "sst")
    p.dma("sync", io["st_out"][:, 1032:1040], sb["last"][:], "sst")
    p.wait_dma("sync", "sst", p.dcnt["sst"])


C_TRUNC = None


def emit_phaseC(p, io, sb):
    def kcs_sl(c):
        return sb["gsig"][:, c // 4, (c % 4) * 128:((c % 4) + 1) * 128]

    sb["kcs_sl"] = kcs_sl
    emit_l1_prologue(p, io, sb, full=True)
    pp = [sb["ps1"][:, 0:TILE], sb["ps2"][:, 0:TILE]]
    psNum = sb["ps4"]
    for t in range(NTILE):
        T0 = t * TILE
        emit_l1_pre(p, io, sb, t, full=True)
        v_sr = p.op("vector", lambda v: v.tensor_copy(sb["stb"], sb["stf"][:]), mark=1)
        p.wait("tensor", "vector", v_sr)
        for tc in (range(NCH) if C_TRUNC != "pre" else []):
            j0 = tc * 128
            def t_kcs(tn, tc=tc):
                for c in range(DC):
                    i = tn.matmul(sb["ps3"][:, c // 4, (c % 4) * 128:(c % 4 + 1) * 128],
                                  sb["kt"][:, tc, c * 128:(c + 1) * 128], sb["u128b"][:],
                                  start=True, stop=True)
                return i

            t_k = p.op("tensor", t_kcs, mark=1)
            p.wait("vector", "tensor", t_k)
            if C_TRUNC == "kcs0":
                p.wait("tensor", "vector", p.mark("vector"))
                continue

            def v_kcs(v, tc=tc):
                v.tensor_copy(sb["gsig"][:, 0:2, :].rearrange("p a b -> p (a b)"),
                              sb["ps3"].rearrange("p a b -> p (a b)"))
                for c in range(DC):
                    i = v.tensor_scalar(sb["kcs_sl"](c), sb["kcs_sl"](c),
                                        sb["ksum"][:, c:c + 1], None, ALU.add)
                return i

            v_k = p.op("vector", v_kcs, mark=1)
            if C_TRUNC == "kcs1":
                p.wait("tensor", "vector", p.mark("vector"))
                continue
            p.wait("scalar", "vector", v_k)
            def s_ks_f(s):
                for c in range(DC):
                    i = s.copy(sb["ksum"][:, c:c + 1], sb["kcs_sl"](c)[:, 127:128])
                return i

            s_ks = p.op("scalar", s_ks_f, mark=1)
            p.wait("vector", "scalar", s_ks)

            if C_TRUNC == "kcs":
                p.wait("tensor", "vector", p.mark("vector"))
                continue
            def v_qk(v, tc=tc, j0=j0):
                for c in range(DC):
                    i = v.tensor_tensor(sb["kcs_sl"](c), sb["qf"][:, c, j0:j0 + 128],
                                        sb["kcs_sl"](c), ALU.mult)
                return i

            v_q = p.op("vector", v_qk, mark=1)
            p.wait("tensor", "vector", v_q)

            def t_den(tn):
                for c in range(DC):
                    i = tn.matmul(sb["ps6"][0:16, 0:128], sb["sel"][:, c * 16:(c + 1) * 16],
                                  sb["kcs_sl"](c), start=(c == 0), stop=(c == DC - 1))
                return i

            t_d = p.op("tensor", t_den, mark=1)
            p.wait("vector", "tensor", t_d)

            def v_den(v):
                v.tensor_scalar(sb["dinv"], sb["ps6"][0:16, 0:128], 1e-5, None, ALU.add)
                return v.reciprocal(sb["dinv"], sb["dinv"])

            v_d = p.op("vector", v_den, mark=1)
            p.wait("tensor", "vector", v_d)
            if C_TRUNC == "den":
                p.wait("tensor", "vector", p.mark("vector"))
                continue

            for c in range(DC):
                def t_sc(tn, c=c, j0=j0):
                    tn.matmul(sb["ps1"][:, 0:128], sb["kf"][0:64, c, j0:j0 + 128],
                              sb["qf"][0:64, c, j0:j0 + 128], start=True, stop=True)
                    return tn.matmul(sb["ps1"][:, 128:256], sb["kf"][:, c, j0:j0 + 128],
                                     sb["qf"][:, c, j0:j0 + 128], start=True, stop=True)

                t_s = p.op("tensor", t_sc, mark=1)
                p.wait("vector", "tensor", t_s)

                def v_w(v, c=c):
                    v.tensor_copy(sb["dh"], sb["ps1"][:, 0:128])
                    v.tensor_tensor(sb["w4"][:, 0, 0:128], sb["dh"], sb["u128"][:], ALU.mult)
                    v.tensor_tensor(sb["tc2"][:, 0:128], sb["ps1"][:, 128:256], sb["dh"], ALU.subtract)
                    return v.tensor_tensor(sb["w4"][:, 0, 128:256], sb["tc2"][:, 0:128],
                                           sb["u128"][:], ALU.mult)

                v_wm = p.op("vector", v_w, mark=1)
                p.wait("tensor", "vector", v_wm)

                def t_num(tn, c=c, tc=tc, j0=j0):
                    dst = psNum[:, c // 4, (c % 4) * 128:(c % 4 + 1) * 128]
                    tn.matmul(dst[0:64, :], sb["vt"][:, tc, (2 * c) * 64:(2 * c + 1) * 64],
                              sb["w4"][:, 0, 0:128], start=True, stop=False, skip_group_check=True)
                    tn.matmul(dst[64:128, :], sb["vt"][:, tc, (2 * c + 1) * 64:(2 * c + 2) * 64],
                              sb["w4"][:, 0, 128:256], start=True, stop=False, skip_group_check=True)
                    i = tn.matmul(dst, sb["stb"][:, c * 128:(c + 1) * 128],
                                  sb["qf"][:, c, j0:j0 + 128], start=False, stop=True,
                                  skip_group_check=True)
                    return i

                tn_last = p.op("tensor", t_num, mark=1)
                p.wait("vector", "tensor", tn_last)
            if C_TRUNC in ("sc", "sc0", "scS"):
                p.wait("tensor", "vector", p.mark("vector"))
                continue
            for hf in range(2):
                def t_su(tn, tc=tc, hf=hf):
                    for cpair in range(4):
                        c = hf * 4 + cpair
                        for par in range(2):
                            h = 2 * c + par
                            r0 = 64 * par
                            i = tn.matmul(sb["ps5"][r0:r0 + 64, cpair * 128 + r0:cpair * 128 + r0 + 64],
                                          sb["kt"][:, tc, h * 64:(h + 1) * 64],
                                          sb["vt"][:, tc, h * 64:(h + 1) * 64],
                                          start=True, stop=True)
                    return i

                t_u = p.op("tensor", t_su, mark=1)
                p.wait("vector", "tensor", t_u)

                def v_su(v, hf=hf):
                    for cpair in range(4):
                        c = hf * 4 + cpair
                        for par in range(2):
                            r0 = 64 * par
                            sl_f = sb["stf"][r0:r0 + 64, c * 128 + r0:c * 128 + r0 + 64]
                            i = v.tensor_tensor(sl_f, sl_f,
                                                sb["ps5"][r0:r0 + 64, cpair * 128 + r0:cpair * 128 + r0 + 64],
                                                ALU.add)
                    return i

                v_u = p.op("vector", v_su, mark=1)
                p.wait("tensor", "vector", v_u)
            v_u2 = p.op("vector", lambda v: v.tensor_copy(sb["stb"], sb["stf"][:]), mark=1)
            p.wait("tensor", "vector", v_u2)

            if C_TRUNC == "su":
                p.wait("tensor", "vector", p.mark("vector"))
                continue
            p.wait("tensor", "vector", v_d)
            for c in range(DC):
                def t_dh(tn, c=c):
                    return tn.matmul(sb["ps6"][:, 128:256], sb["selt"][0:16, c * 128:(c + 1) * 128],
                                     sb["dinv"], start=True, stop=True)

                t_dhm = p.op("tensor", t_dh, mark=1)
                p.wait("vector", "tensor", t_dhm)

                def v_out(v, c=c, j0=j0):
                    v.tensor_copy(sb["dh"], sb["ps6"][:, 128:256])
                    return v.tensor_tensor(sb["ao"][:, c, j0:j0 + 128],
                                           psNum[:, c // 4, (c % 4) * 128:(c % 4 + 1) * 128],
                                           sb["dh"], ALU.mult)

                v_o = p.op("vector", v_out, mark=1)
                p.wait("tensor", "vector", v_o)

        if C_TRUNC in ("pre", "chunks", "kcs", "kcs0", "kcs1", "den", "sc", "sc0", "scS", "su"):
            vf0 = p.mark("vector")
            p.wait("sync", "vector", vf0)
            p.dma("sync", fm(io["out_fm"], T0, T0 + TILE), sb["x1f"][:], "sout")
            p.wait("sync", "tensor", p.mark("tensor"))
            continue
        p.wait("gpsimd", "tensor", p.mark("tensor"))
        lw_p = p.dma("gpsimd", sb["wproj"][:], wlay(io["w_proj"]), "lws")
        at = sb["gn"]
        p.wait_dma("tensor", "lws", lw_p)
        for co in range(DC):
            ps = pp[co % 2]

            def t_pj(tn, co=co, ps=ps):
                for c in range(DC):
                    i = tn.matmul(ps, sb["wproj"][:, c, co * 128:(co + 1) * 128],
                                  sb["ao"][:, c, :], start=(c == 0), stop=(c == DC - 1))
                return i

            t_p = p.op("tensor", t_pj, mark=1)
            p.wait("vector", "tensor", t_p)
            v_p = p.op("vector", lambda v, co=co, ps=ps: v.tensor_scalar(
                at[:, co, :], ps, sb["bproj"][:, co:co + 1], None, ALU.add), mark=1)
            p.wait("tensor", "vector", v_p)

        def v_mix(v):
            for c in range(DC):
                v.tensor_sub(sb["tc1"][:, 1:TILE], sb["xn1b"][:, c, 1:TILE], sb["xn1b"][:, c, 0:TILE - 1])
                v.tensor_tensor(sb["tc1"][:, 0:1], sb["xn1b"][:, c, 0:1], sb["carry"][:, c:c + 1], ALU.subtract)
                v.scalar_tensor_tensor(sb["xrb"][:, c, 1:TILE], sb["tc1"][:, 1:TILE],
                                       sb["mu_r"][:, c:c + 1], sb["xn1b"][:, c, 0:TILE - 1],
                                       ALU.mult, ALU.add)
                v.scalar_tensor_tensor(sb["xrb"][:, c, 0:1], sb["tc1"][:, 0:1],
                                       sb["mu_r"][:, c:c + 1], sb["carry"][:, c:c + 1],
                                       ALU.mult, ALU.add)
                v.scalar_tensor_tensor(sb["xkb"][:, c, 1:TILE], sb["tc1"][:, 1:TILE],
                                       sb["mu_k"][:, c:c + 1], sb["xn1b"][:, c, 0:TILE - 1],
                                       ALU.mult, ALU.add)
                i = v.scalar_tensor_tensor(sb["xkb"][:, c, 0:1], sb["tc1"][:, 0:1],
                                           sb["mu_k"][:, c:c + 1], sb["carry"][:, c:c + 1],
                                           ALU.mult, ALU.add)
            return i

        v_m = p.op("vector", v_mix, mark=1)
        p.wait("scalar", "vector", v_m)
        s_c = p.op("scalar", lambda s: s.copy(
            sb["carry"][:], sb["xn1b"][:, :, TILE - 1:TILE].rearrange("p c o -> p (c o)")), mark=1)
        p.wait("vector", "scalar", s_c)
        p.wait("tensor", "vector", v_m)

        p.wait("gpsimd", "tensor", p.mark("tensor"))
        lw_r = p.dma("gpsimd", sb["wr"][:], wlay(io["wr_t"]), "lws")
        p.wait_dma("tensor", "lws", lw_r)
        for co in range(DC):
            ps = pp[co % 2]

            def t_r(tn, co=co, ps=ps):
                for c in range(DC):
                    i = tn.matmul(ps, sb["wr"][:, c, co * 128:(co + 1) * 128],
                                  sb["xrb"][:, c, :], start=(c == 0), stop=(c == DC - 1))
                return i

            t_rm = p.op("tensor", t_r, mark=1)
            p.wait("scalar", "tensor", t_rm)
            s_e = p.op("scalar", lambda s, ps=ps: s.activation(
                sb["tc1"][:], ps, AF.Exp, scale=-1.0), mark=1)
            p.wait("vector", "scalar", s_e)

            def v_r(v, co=co):
                v.tensor_scalar_add(sb["tc1"][:], sb["tc1"][:], 1.0)
                return v.reciprocal(sb["gsig"][:, co, :], sb["tc1"][:])

            v_rm = p.op("vector", v_r, mark=1)
            p.wait("tensor", "vector", v_rm)

        p.wait("gpsimd", "tensor", p.mark("tensor"))
        lw_k = p.dma("gpsimd", sb["wk1"][:], wlay(io["wk1_t"]), "lws")
        p.wait_dma("tensor", "lws", lw_k)
        for co in range(DC):
            ps = pp[co % 2]

            def t_ka(tn, co=co, ps=ps):
                for c in range(DC):
                    i = tn.matmul(ps, sb["wk1"][:, c, co * 128:(co + 1) * 128],
                                  sb["xkb"][:, c, :], start=(c == 0), stop=(c == DC - 1))
                return i

            t_kam = p.op("tensor", t_ka, mark=1)
            p.wait("vector", "tensor", t_kam)

            def v_ka(v, co=co, ps=ps):
                v.tensor_scalar(sb["tc1"][:], ps, 0.0, None, ALU.max)
                return v.tensor_tensor(sb["kab"][:, co, :], sb["tc1"][:], sb["tc1"][:], ALU.mult)

            v_kam = p.op("vector", v_ka, mark=1)
            p.wait("tensor", "vector", v_kam)

        p.wait("gpsimd", "tensor", p.mark("tensor"))
        lw_v = p.dma("gpsimd", sb["wv1"][:], wlay(io["wv1_t"]), "lws")
        p.wait_dma("tensor", "lws", lw_v)
        for co in range(DC):
            ps = pp[co % 2]

            def t_f(tn, co=co, ps=ps):
                for c in range(DC):
                    i = tn.matmul(ps, sb["wv1"][:, c, co * 128:(co + 1) * 128],
                                  sb["kab"][:, c, :], start=(c == 0), stop=(c == DC - 1))
                return i

            t_fm = p.op("tensor", t_f, mark=1)
            p.wait("vector", "tensor", t_fm)

            def v_f(v, co=co, ps=ps):
                v.tensor_tensor(sb["tc1"][:], ps, sb["gsig"][:, co, :], ALU.mult)
                v.tensor_tensor(sb["tc2"][:], sb["x1f"][:, co, :], at[:, co, :], ALU.add)
                return v.tensor_tensor(sb["x1f"][:, co, :], sb["tc2"][:], sb["tc1"][:], ALU.add)

            v_fm2 = p.op("vector", v_f, mark=1)
            p.wait("tensor", "vector", v_fm2)
        vf = p.mark("vector")
        p.wait("sync", "vector", vf)
        p.dma("sync", fm(io["out_fm"], T0, T0 + TILE), sb["x1f"][:], "sout")
        p.wait("sync", "tensor", p.mark("tensor"))
    p.wait_dma("sync", "sout", p.dcnt["sout"])


def sel_arrays():
    sel = np.zeros((128, DC * 16), np.float32)
    selt = np.zeros((16, DC * 128), np.float32)
    for c in range(DC):
        for pp_ in range(128):
            h = c * 2 + (pp_ // 64)
            sel[pp_, c * 16 + h] = 1.0
            selt[h, c * 128 + pp_] = 1.0
    return sel, selt


L1_COMMON = [("x1s", [D, NTOK], F32), ("ln_g", [128, DC], F32), ("ln_b", [128, DC], F32),
             ("w_qkv", [D, 3 * D], F32), ("b_qkv_col", [128, 3 * DC], F32),
             ("b_qkv_row", [1, 3 * D], F32), ("w_gate", [D, D], F32), ("b_gate", [128, DC], F32),
             ("i128", [128, 128], F32), ("u128", [128, 128], F32),
             ("consts", [128, len(CONST_VALS)], F32)]
C_EXTRA = [("w_proj", [D, D], F32), ("b_proj", [128, DC], F32),
           ("wr_t", [D, D], F32), ("wk1_t", [D, D], F32), ("wv1_t", [D, D], F32),
           ("mu_r", [128, DC], F32), ("mu_k", [128, DC], F32),
           ("sel", [128, DC * 16], F32), ("selt", [16, DC * 128], F32),
           ("stprev", [128, STW], F32)]


def build_BP():
    nc = bass.Bass("TRN2", target_bir_lowering=False)
    io = {}
    for n, shp, dt in L1_COMMON:
        io[n] = nc.dram_tensor(n, shp, dt, kind="ExternalInput").ap()
    io["st_out"] = nc.dram_tensor("st_out", [128, STW], F32, kind="ExternalOutput").ap()
    with ExitStack() as stack:
        sb = alloc_phaseL1(nc, stack, full=False)
        for i, v in enumerate(CONST_VALS):
            nc.const_aps.register(F32, v, sb["consts"][:, i:i + 1])
        p = Prog(nc, stack)
        emit_phaseBP(p, io, sb)
        block = stack.enter_context(nc.Block())
        p.emit(block)
    return nc


def build_C():
    nc = bass.Bass("TRN2", target_bir_lowering=False)
    io = {}
    for n, shp, dt in L1_COMMON + C_EXTRA:
        io[n] = nc.dram_tensor(n, shp, dt, kind="ExternalInput").ap()
    io["out_fm"] = nc.dram_tensor("out_fm", [D, NTOK], F32, kind="ExternalOutput").ap()
    with ExitStack() as stack:
        sb = alloc_phaseL1(nc, stack, full=True)
        for i, v in enumerate(CONST_VALS):
            nc.const_aps.register(F32, v, sb["consts"][:, i:i + 1])
        p = Prog(nc, stack)
        emit_phaseC(p, io, sb)
        block = stack.enter_context(nc.Block())
        p.emit(block)
    return nc


def kernel(**inputs):
    inputs = {k: np.asarray(v) for k, v in inputs.items()}
    g, per = host_arrays(inputs)

    def maps(names):
        out = []
        for r in range(NCORES):
            m = {n: g[n] for n in names if n in g}
            m["x_fm"] = per[r]["x_fm"]
            out.append(m)
        return out

    ncA = build_A()
    mapsA = maps([n for n, _ in A_INPUT_SHAPES])
    resA = run_bass_kernel_spmd(ncA, mapsA, core_ids=list(range(NCORES)))

    ncB = build_B()
    mapsB = maps([n for n, _, _ in B_INPUT_SHAPES])
    z0 = np.zeros((M0, M0), np.float32)
    for r in range(NCORES):
        ra = resA.results[r]
        mapsB[r]["xn0s"] = np.asarray(ra["xn0s"])
        mapsB[r]["reads_s"] = np.asarray(ra["reads_s"])
        mapsB[r]["qlam_s"] = np.asarray(ra["qlam_s"])
        mapsB[r]["s0prev"] = np.asarray(resA.results[r - 1]["s0"]) if r % 2 == 1 else z0
    resB = run_bass_kernel_spmd(ncB, mapsB, core_ids=list(range(NCORES)))

    ncBP = build_BP()
    mapsP = maps([n for n, _, _ in L1_COMMON])
    for r in range(NCORES):
        mapsP[r]["x1s"] = np.asarray(resB.results[r]["x1s"])
    resP = run_bass_kernel_spmd(ncBP, mapsP, core_ids=list(range(NCORES)))

    ncC = build_C()
    mapsC = maps([n for n, _, _ in L1_COMMON + C_EXTRA])
    zs = np.zeros((128, STW), np.float32)
    for r in range(NCORES):
        mapsC[r]["x1s"] = np.asarray(resB.results[r]["x1s"])
        mapsC[r]["stprev"] = np.asarray(resP.results[r - 1]["st_out"]) if r % 2 == 1 else zs
    resC = run_bass_kernel_spmd(ncC, mapsC, core_ids=list(range(NCORES)))

    out = np.empty((B, T, D), np.float32)
    for r in range(NCORES):
        b, h = r // 2, r % 2
        out[b, h * NTOK:(h + 1) * NTOK, :] = np.asarray(resC.results[r]["out_fm"]).T
    return out


if __name__ == "__main__":
    nc = build_A()
    print("built phase A ok")
    nc = build_B()
    print("built phase B ok")
    nc = build_BP()
    print("built phase B' ok")
    nc = build_C()
    print("built phase C ok")
